# version 13 (final)
# baseline (speedup 1.0000x reference)
import contextlib
import os
import numpy as np

os.environ.setdefault("TRN_ALLOW_ICI_ALL_TO_ALL", "1")

import concourse.bass as bass
import concourse.mybir as mybir
from concourse.bass_utils import run_bass_kernel_spmd
from concourse.aot_env import allow_ici_all_to_all

F32 = mybir.dt.float32
BF16 = mybir.dt.bfloat16
I32 = mybir.dt.int32
AF = mybir.ActivationFunctionType
OP = mybir.AluOpType
AX = mybir.AxisListType.X

NCORE = 8
H = 4096
DC = 1024
NST = 16
NBLK = 8
TB = 1024
HB = 512
KTH = 32
DT = 8
EPS_RMS = 1e-6
EPS_LN = 1e-5

DEBUG_TAPS = False
PIPE_INTERLEAVE = True
PIPE_K = 10**9
PIPE_AR = True
YACC_G = True

ENGINES = ("sync", "gpsimd", "scalar", "vector", "tensor")


class Slot:
    __slots__ = ("w", "r")

    def __init__(self):
        self.w = []
        self.r = []


class Bld:
    def __init__(self, nc, stack):
        self.nc = nc
        self.ops = {e: [] for e in ENGINES}
        self.sems = {}
        self.count = {}
        self.water = {}
        self.meta = {e: [] for e in ENGINES}
        names = ["tv", "ts", "tt", "tg"]
        self.lanes = {}
        for q in ("dS", "dA", "dG"):
            self.lanes[q] = [f"{q}{i}" for i in range(4)]
            names.extend(self.lanes[q])
        self.qidx = {q: 0 for q in self.lanes}
        for name in names:
            self.sems[name] = stack.enter_context(nc.semaphore(name))
            self.count[name] = 0

    def _wait(self, eng, dep):
        if dep is None:
            return
        name, cnt = dep
        if cnt <= 0:
            return
        key = (eng, name)
        if self.water.get(key, 0) >= cnt:
            return
        self.water[key] = cnt
        h = self.sems[name]
        self.meta[eng].append(("w", name, cnt))
        self.ops[eng].append(lambda e, h=h, c=cnt: e.wait_ge(h, c))

    def emit(self, eng, sem, by, fn, rs=(), ws=(), deps=()):
        reads = []
        for s in rs:
            reads.extend(s.w)
        for s in ws:
            reads.extend(s.w)
            reads.extend(s.r)
        reads.extend(deps)
        for d in reads:
            self._wait(eng, d)
        self.count[sem] += by
        tgt = self.count[sem]
        h = self.sems[sem]

        def run(e, f=fn, h=h, v=by):
            f(e).then_inc(h, v)

        self.meta[eng].append(("i", sem, by))
        self.ops[eng].append(run)
        tok = (sem, tgt)
        for s in rs:
            s.r.append(tok)
        for s in ws:
            s.w.append(tok)
            s.r = []
        return tok

    def v(self, fn, rs=(), ws=(), deps=()):
        return self.emit("vector", "tv", 1, fn, rs, ws, deps)

    def s(self, fn, rs=(), ws=(), deps=()):
        return self.emit("scalar", "ts", 1, fn, rs, ws, deps)

    def t(self, fn, rs=(), ws=(), deps=()):
        return self.emit("tensor", "tt", 1, fn, rs, ws, deps)

    def g(self, fn, rs=(), ws=(), deps=()):
        return self.emit("gpsimd", "tg", 1, fn, rs, ws, deps)

    def _dma(self, eng, q, fn, rs, ws, deps):
        lane = self.lanes[q][self.qidx[q] % 4]
        self.qidx[q] += 1
        deps = tuple(deps) + ((lane, self.count[lane]),)
        return self.emit(eng, lane, 16, fn, rs, ws, deps)

    def dS(self, fn, rs=(), ws=(), deps=()):
        return self._dma("sync", "dS", fn, rs, ws, deps)

    def dA(self, fn, rs=(), ws=(), deps=()):
        return self._dma("scalar", "dA", fn, rs, ws, deps)

    def dG(self, fn, rs=(), ws=(), deps=()):
        return self._dma("gpsimd", "dG", fn, rs, ws, deps)

    def tgroup(self, fns, rs=(), ws=(), deps=()):
        reads = []
        for s in rs:
            reads.extend(s.w)
        for s in ws:
            reads.extend(s.w)
            reads.extend(s.r)
        reads.extend(deps)
        for d in reads:
            self._wait("tensor", d)
        for fn in fns[:-1]:
            self.ops["tensor"].append(fn)
            self.meta["tensor"].append(("n",))
        self.count["tt"] += 1
        tgt = self.count["tt"]
        h = self.sems["tt"]

        def run(e, f=fns[-1], h=h):
            f(e).then_inc(h, 1)

        self.meta["tensor"].append(("i", "tt", 1))
        self.ops["tensor"].append(run)
        tok = ("tt", tgt)
        for s in rs:
            s.r.append(tok)
        for s in ws:
            s.w.append(tok)
            s.r = []
        return tok

    def fence(self):
        snap = [(n, c) for n, c in self.count.items() if c > 0]
        for eng in ENGINES:
            for d in snap:
                self._wait(eng, d)


def build_program(taps=False):
    nc = bass.Bass("TRN2", target_bir_lowering=False, debug=False)

    def din(name, shape, dt=F32):
        return nc.dram_tensor(name, shape, dt, kind="ExternalInput").ap()

    hT = din("hT", [H, NBLK * TB])
    resid = din("resid", [NBLK * 128, H])
    win = din("win", [H, 2048])
    wout = din("wout", [2 * H, H])
    wffnT = din("wffnT", [H, H])
    convw = din("convw", [128, DT * 4])
    convb = din("convb", [128, DT])
    dtbias = din("dtbias", [128, DT])
    Dp = din("Dp", [128, DT])
    Alog = din("Alog", [128, 128])
    xpw = din("xpw", [DC, 288])
    dtw = din("dtw", [256, DC])
    dtlnw = din("dtlnw", [128, 2])
    bcsel_in = din("bcsel", [48, 2048])
    blnrep = din("blnrep", [128, 32])
    ffnb_in = din("ffnb", [128, H])
    kk_in = din("kk", [128, 1])
    aa_in = din("aa", [128, 1])

    out = nc.dram_tensor("out", [NBLK * 128, H], F32, kind="ExternalOutput").ap()

    win_bf = nc.dram_tensor("win_bf", [H, 2048], BF16, kind="Internal").ap()
    wffn_bf = nc.dram_tensor("wffn_bf", [H, H], BF16, kind="Internal").ap()
    zsil = nc.dram_tensor("zsil", [NBLK, DT, 128, TB], BF16, kind="Internal").ap()
    arin = nc.dram_tensor("arin", [2 * NBLK, HB, 288], F32, kind="Internal").ap()
    arout = nc.dram_tensor("arout", [2 * NBLK, HB, 288], F32, kind="Internal").ap()
    y3a_in = nc.dram_tensor("y3a_in", [NBLK, NCORE, DC, 128], BF16, kind="Internal").ap()
    y3a_out = nc.dram_tensor("y3a_out", [NBLK, NCORE, DC, 128], BF16, kind="Internal").ap()
    y3p = nc.dram_tensor("y3p", [2, 64 * 128, 512], BF16, kind="Internal").ap()
    x1p = nc.dram_tensor("x1p", [KTH * 128, 512], BF16, kind="Internal").ap()
    x1sp = nc.dram_tensor("x1sp", [4, 128, H], F32, kind="Internal").ap()
    attnp = nc.dram_tensor("attnp", [4, 128, H], BF16, kind="Internal").ap()
    x2p = nc.dram_tensor("x2p", [4, 128, H], F32, kind="Internal").ap()

    tap_t = {}
    if taps:
        tap_t["xc0"] = nc.dram_tensor("tap_xc0", [DT * 128, TB], BF16, kind="ExternalOutput").ap()
        tap_t["dt0"] = nc.dram_tensor("tap_dt0", [DT * 128, TB], BF16, kind="ExternalOutput").ap()
        tap_t["ssmp0"] = nc.dram_tensor("tap_ssmp0", [TB, 288], F32, kind="ExternalOutput").ap()
        tap_t["y0"] = nc.dram_tensor("tap_y0", [DT * 128, HB], F32, kind="ExternalOutput").ap()
        tap_t["y30"] = nc.dram_tensor("tap_y30", [DC, TB], BF16, kind="ExternalOutput").ap()
        tap_t["x10"] = nc.dram_tensor("tap_x10", [128, H], F32, kind="ExternalOutput").ap()
        tap_t["y3o"] = nc.dram_tensor("tap_y3o", [NCORE * DC, 128], BF16, kind="ExternalOutput").ap()
        tap_t["at0"] = nc.dram_tensor("tap_at0", [128, H], BF16, kind="ExternalOutput").ap()
        tap_t["wb0"] = nc.dram_tensor("tap_wb0", [256, H], BF16, kind="ExternalOutput").ap()
        tap_t["x2p0"] = nc.dram_tensor("tap_x2p0", [128, H], F32, kind="ExternalOutput").ap()
        tap_t["ps0"] = nc.dram_tensor("tap_ps0", [4, 128, 512], F32, kind="ExternalOutput").ap()
        tap_t["wbl"] = nc.dram_tensor("tap_wbl", [256, H], BF16, kind="ExternalOutput").ap()
        tap_t["x1pl"] = nc.dram_tensor("tap_x1pl", [256, 512], BF16, kind="ExternalOutput").ap()

    big = contextlib.ExitStack()
    with big:
        bld = Bld(nc, big)
        ec = big.enter_context

        sb_convw = ec(nc.sbuf_tensor("c_convw", [128, DT * 4], F32))
        sb_convb = ec(nc.sbuf_tensor("c_convb", [128, DT], F32))
        sb_dtbias = ec(nc.sbuf_tensor("c_dtbias", [128, DT], F32))
        sb_Dp = ec(nc.sbuf_tensor("c_Dp", [128, DT], F32))
        sb_A = ec(nc.sbuf_tensor("c_A", [128, 128], F32))
        sb_xpw = ec(nc.sbuf_tensor("c_xpw", [128, DT * 288], BF16))
        sb_dtw = ec(nc.sbuf_tensor("c_dtw", [128, 2 * DC], BF16))
        sb_dtln = ec(nc.sbuf_tensor("c_dtln", [128, 2], F32))
        sb_bcsel = ec(nc.sbuf_tensor("c_bcsel", [48, 2048], BF16))
        sb_bln = ec(nc.sbuf_tensor("c_bln", [128, 32], F32))
        sb_eye = ec(nc.sbuf_tensor("c_eye", [128, 128], F32))
        sb_idn = ec(nc.sbuf_tensor("c_idn", [128, 128], I32))
        sb_kk = ec(nc.sbuf_tensor("c_kk", [128, 1], F32))
        sb_eps = ec(nc.sbuf_tensor("c_eps", [128, 3], F32))
        sb_aa = ec(nc.sbuf_tensor("c_aa", [128, 1], F32))

        sb_xc = ec(nc.sbuf_tensor("b_xc", [128, DT * TB], BF16))
        sb_dtx = ec(nc.sbuf_tensor("b_dtx", [128, DT * HB], BF16))
        sb_dt = ec(nc.sbuf_tensor("b_dt", [128, DT * HB], BF16))
        sb_y = ec(nc.sbuf_tensor("b_y", [128, DT * HB], F32))
        sb_zrd = ec(nc.sbuf_tensor("b_zrd", [128, DT * HB], BF16))
        sb_ssmp = ec(nc.sbuf_tensor("b_ssmp", [128, 4 * 288], F32))
        sb_dtrT = ec(nc.sbuf_tensor("b_dtrT", [128, 2 * TB], BF16))
        sb_bct = ec(nc.sbuf_tensor("b_bct", [48, TB], BF16))
        sb_sq = ec(nc.sbuf_tensor("b_sq", [128, 256], F32))
        sb_st = ec(nc.sbuf_tensor("b_st", [128, 16], F32))
        sb_dtr = ec(nc.sbuf_tensor("b_dtr", [128, 256], F32))
        sb_bc = ec(nc.sbuf_tensor("b_bc", [128, 32], F32))
        sb_hcar = ec(nc.sbuf_tensor("b_hcar", [128, 128], F32))
        sb_a = ec(nc.sbuf_tensor("b_a", [128, 2 * HB], F32))
        sb_b = ec(nc.sbuf_tensor("b_b", [128, HB], BF16))
        sb_h = ec(nc.sbuf_tensor("b_h", [128, 3 * HB], BF16))
        sb_g = ec(nc.sbuf_tensor("b_g", [128, 3 * HB], BF16))
        sb_brep = ec(nc.sbuf_tensor("b_brep", [128, 3 * HB], BF16))
        sb_crep = ec(nc.sbuf_tensor("b_crep", [128, 3 * HB], BF16))
        sb_cvt = ec(nc.sbuf_tensor("b_cvt", [128, 2 * HB], F32))
        sb_sp1 = ec(nc.sbuf_tensor("b_sp1", [128, 2 * HB], F32))

        psum = ec(nc.psum_tensor("ps", [128, 4096], F32))

        def BK(b, w=512):
            return psum[:, b * 512:b * 512 + w]

        S = Slot
        sl_const = S()
        sl_win = [S() for _ in range(16)]
        sl_wffn = [S() for _ in range(KTH)]
        sl_xc = [[S(), S()] for _ in range(DT)]
        sl_dtx = [S() for _ in range(DT)]
        sl_dt = [S() for _ in range(DT)]
        sl_y = [S() for _ in range(DT)]
        sl_zrd = [S() for _ in range(DT)]
        sl_ssmp = S()
        sl_dtrT = [S(), S()]
        sl_bct = [S(), S()]
        sl_sq = S()
        sl_st = S()
        sl_dtr = S()
        sl_bc = S()
        sl_hcar = S()
        sl_a = [S(), S()]
        sl_b = S()
        sl_h = [S(), S(), S()]
        sl_g = [S(), S(), S()]
        sl_brep = [S(), S(), S()]
        sl_crep = [S(), S(), S()]
        sl_cvt = [S(), S()]
        sl_sp1 = [S(), S()]
        sl_bank = [S() for _ in range(8)]
        sl_arin = [S() for _ in range(2 * NBLK)]
        sl_arout = [S() for _ in range(2 * NBLK)]
        sl_y3in = [S() for _ in range(NBLK)]
        sl_y3out = [S() for _ in range(NBLK)]
        sl_y3p = [S(), S()]
        sl_zsil = [[S() for _ in range(DT)] for _ in range(NBLK)]
        sl_x1p = [S() for _ in range(KTH)]
        sl_x1sp = [S() for _ in range(4)]
        sl_attnp = [S() for _ in range(4)]
        sl_x2p = [S() for _ in range(4)]

        bld.v(lambda e: e.memset(sb_eps[:, 0:1], EPS_RMS), ws=(sl_const,))
        bld.v(lambda e: e.memset(sb_eps[:, 1:2], EPS_LN), ws=(sl_const,))
        bld.v(lambda e: e.memset(sb_eps[:, 2:3], 1.0), ws=(sl_const,))
        bld.dS(lambda e: e.dma_start(out=sb_convw[:, :], in_=convw[:, :]), ws=(sl_const,))
        bld.dS(lambda e: e.dma_start(out=sb_convb[:, :], in_=convb[:, :]), ws=(sl_const,))
        bld.dS(lambda e: e.dma_start(out=sb_dtbias[:, :], in_=dtbias[:, :]), ws=(sl_const,))
        bld.dS(lambda e: e.dma_start(out=sb_Dp[:, :], in_=Dp[:, :]), ws=(sl_const,))
        bld.dS(lambda e: e.dma_start(out=sb_dtln[:, :], in_=dtlnw[:, :]), ws=(sl_const,))
        bld.dS(lambda e: e.dma_start(out=sb_bln[:, :], in_=blnrep[:, :]), ws=(sl_const,))
        bld.dS(lambda e: e.dma_start(out=sb_kk[:, :], in_=kk_in[:, :]), ws=(sl_const,))
        bld.dS(lambda e: e.dma_start(out=sb_aa[:, :], in_=aa_in[:, :]), ws=(sl_const,))
        bld.dS(lambda e: e.dma_start(out=sb_cvt[:, 0:128], in_=Alog[:, :]), ws=(sl_cvt[0],))
        bld.s(lambda e: e.activation(out=sb_A[:, :], in_=sb_cvt[:, 0:128], func=AF.Exp),
              rs=(sl_cvt[0],), ws=(sl_const,))
        bld.v(lambda e: e.tensor_scalar(out=sb_A[:, :], in0=sb_A[:, :], scalar1=-1.0,
                                        scalar2=None, op0=OP.mult), ws=(sl_const,))
        bld.dG(lambda e: e.dma_start(out=sb_xpw.ap().rearrange("p (k e) -> p k e", k=DT),
                                     in_=xpw.rearrange("(k p) e -> p k e", p=128)),
               ws=(sl_const,))
        for kt in range(2):
            bld.dS(lambda e, kt=kt: e.dma_start(
                out=sb_cvt[:, 0:1024],
                in_=dtw.rearrange("(k p) m -> p k m", p=128)[:, kt:kt + 1, :]),
                ws=(sl_cvt[0],))
            bld.v(lambda e, kt=kt: e.tensor_scalar(
                out=sb_dtw[:, kt * DC:(kt + 1) * DC], in0=sb_cvt[:, 0:1024],
                scalar1=sb_dtln[:, kt:kt + 1], scalar2=None, op0=OP.mult),
                rs=(sl_cvt[0], sl_const), ws=(sl_const,))
        bld.dG(lambda e: e.dma_start(out=sb_bcsel[:, :], in_=bcsel_in[:, :]), ws=(sl_const,))
        bld.g(lambda e: e.iota(sb_idn[:, :], [[1, 128]], base=0, channel_multiplier=-1),
              ws=(sl_const,))
        bld.v(lambda e: e.tensor_single_scalar(sb_eye[:, :], sb_idn[:, :], 0, OP.is_equal),
              rs=(sl_const,), ws=(sl_const,))
        for m in range(16):
            bld.dG(lambda e, m=m: e.dma_start(out=win_bf[:, m * 128:(m + 1) * 128],
                                              in_=win[:, m * 128:(m + 1) * 128]),
                   ws=(sl_win[m],))

        state = {"ip": None}

        ip_gen = [0]

        def alloc_inproj():
            st = contextlib.ExitStack()
            gg = ip_gen[0]
            ip_gen[0] += 1
            t = {
                "hT": [st.enter_context(nc.sbuf_tensor(f"ip_hT{q}_{gg}", [128, KTH * HB], BF16))
                       for q in range(2)],
                "W": [st.enter_context(nc.sbuf_tensor(f"ip_W{q}_{gg}", [128, KTH * 128], BF16))
                      for q in range(2)],
                "xe": st.enter_context(nc.sbuf_tensor(f"ip_xe_{gg}", [128, DT * (TB + 3)], BF16)),
                "zt": [st.enter_context(nc.sbuf_tensor(f"ip_zt{q}_{gg}", [128, HB], BF16))
                       for q in range(2)],
            }
            s = {"hT": [S(), S()], "W": [S(), S()], "xe": S(), "zt": [S(), S()]}
            state["ip"] = (st, t, s)
            for d in range(DT):
                bld.v(lambda e, d=d, t=t: e.memset(t["xe"][:, d * (TB + 3):d * (TB + 3) + 3], 0.0),
                      ws=(s["xe"],))

        def free_inproj():
            st, _, _ = state["ip"]
            st.close()
            state["ip"] = None

        def load_hT(blk):
            _, t, s = state["ip"]
            for tq in range(2):
                col0 = blk * TB + tq * HB
                bld.dG(lambda e, tq=tq, col0=col0, t=t: e.dma_start(
                    out=t["hT"][tq].ap().rearrange("p (k t) -> p k t", k=KTH),
                    in_=hT[:, col0:col0 + HB].rearrange("(k p) t -> p k t", p=128)),
                    ws=(s["hT"][tq],))

        def prefetch_w(m):
            _, t, s = state["ip"]
            bld.dS(lambda e, m=m, t=t: e.dma_start(
                out=t["W"][m % 2].ap().rearrange("p (k c) -> p k c", k=KTH),
                in_=win_bf[:, m * 128:(m + 1) * 128].rearrange("(k p) c -> p k c", p=128)),
                rs=(sl_win[m],), ws=(s["W"][m % 2],))

        def emit_inproj_mgroup(blk, m):
            _, t, s = state["ip"]
            if m + 1 < 16:
                mm = m + 1
                bld.dS(lambda e, mm=mm, t=t: e.dma_start(
                    out=t["W"][mm % 2].ap().rearrange("p (k c) -> p k c", k=KTH),
                    in_=win_bf[:, mm * 128:(mm + 1) * 128].rearrange("(k p) c -> p k c", p=128)),
                    rs=(sl_win[mm],), ws=(s["W"][mm % 2],))
            for tq in range(2):
                fns = []
                for kt in range(KTH):
                    fns.append(lambda e, kt=kt, m=m, tq=tq, t=t: e.matmul(
                        BK(tq),
                        t["W"][m % 2][:, kt * 128:(kt + 1) * 128],
                        t["hT"][tq][:, kt * HB:(kt + 1) * HB],
                        start=(kt == 0), stop=(kt == KTH - 1)))
                bld.tgroup(fns, rs=(s["W"][m % 2], s["hT"][tq]), ws=(sl_bank[tq],))
                if m < 8:
                    off = m * (TB + 3) + 3 + tq * HB
                    bld.s(lambda e, off=off, tq=tq, t=t: e.activation(
                        out=t["xe"][:, off:off + HB], in_=BK(tq), func=AF.Copy),
                        rs=(sl_bank[tq],), ws=(s["xe"],))
                else:
                    zi = tq
                    bld.s(lambda e, tq=tq, t=t: e.activation(
                        out=t["zt"][tq][:, :], in_=BK(tq), func=AF.Silu),
                        rs=(sl_bank[tq],), ws=(s["zt"][tq],))
                    bld.dA(lambda e, blk=blk, m=m, tq=tq, t=t: e.dma_start(
                        out=zsil[blk, m - 8, :, tq * HB:(tq + 1) * HB],
                        in_=t["zt"][tq][:, :]),
                        rs=(s["zt"][tq],), ws=(sl_zsil[blk][m - 8],))

        def emit_inproj(blk):
            prefetch_w(0)
            for m in range(16):
                emit_inproj_mgroup(blk, m)

        def pc_conv(blk, half, d):
            _, t, s = state["ip"]
            xe, sx = t["xe"], s["xe"]
            base = d * (TB + 3) + half * HB
            cpar = (half * DT + d) % 2
            cvt = sb_cvt[:, cpar * HB:(cpar + 1) * HB]
            bld.v(lambda e, d=d, base=base, cvt=cvt, xe=xe: e.tensor_scalar(
                out=cvt, in0=xe[:, base:base + HB],
                scalar1=sb_convw[:, d * 4 + 0:d * 4 + 1],
                scalar2=sb_convb[:, d:d + 1], op0=OP.mult, op1=OP.add),
                rs=(sx, sl_const), ws=(sl_cvt[cpar],))
            for k in range(1, 4):
                bld.v(lambda e, d=d, k=k, base=base, cvt=cvt, xe=xe: e.scalar_tensor_tensor(
                    out=cvt, in0=xe[:, base + k:base + k + HB],
                    scalar=sb_convw[:, d * 4 + k:d * 4 + k + 1],
                    in1=cvt, op0=OP.mult, op1=OP.add),
                    rs=(sx, sl_const), ws=(sl_cvt[cpar],))
            bld.s(lambda e, d=d, half=half, cvt=cvt: e.activation(
                out=sb_xc[:, d * TB + half * HB:d * TB + (half + 1) * HB],
                in_=cvt, func=AF.Silu),
                rs=(sl_cvt[cpar],), ws=(sl_xc[d][half],))

        def pc_shift(d):
            _, t, s = state["ip"]
            xe, sx = t["xe"], s["xe"]
            base = d * (TB + 3)
            bld.v(lambda e, base=base, xe=xe: e.tensor_copy(
                xe[:, base:base + 3], xe[:, base + TB:base + TB + 3]),
                rs=(sx,), ws=(sx,))

        def pc_xproj(blk, half, tt):
            fns = []
            for kt in range(DT):
                fns.append(lambda e, kt=kt, tt=tt, half=half: e.matmul(
                    BK(2, 288),
                    sb_xc[:, kt * TB + half * HB + tt * 128:kt * TB + half * HB + (tt + 1) * 128],
                    sb_xpw[:, kt * 288:(kt + 1) * 288],
                    start=(kt == 0), stop=(kt == DT - 1)))
            bld.tgroup(fns, rs=tuple(sl_xc[kt][half] for kt in range(DT)) + (sl_const,),
                       ws=(sl_bank[2],))
            bld.s(lambda e, tt=tt: e.activation(
                out=sb_ssmp[:, tt * 288:(tt + 1) * 288], in_=BK(2, 288), func=AF.Copy),
                rs=(sl_bank[2],), ws=(sl_ssmp,))

        def pc_arw(blk, half):
            u = 2 * blk + half
            bld.dS(lambda e, u=u: e.dma_start(
                out=arin[u].rearrange("(g p) e -> p g e", p=128),
                in_=sb_ssmp.ap().rearrange("p (g e) -> p g e", g=4)),
                rs=(sl_ssmp,), ws=(sl_arin[u],))

        def pc_ar(blk, half):
            u = 2 * blk + half
            bld.g(lambda e, u=u: e.collective_compute(
                "AllReduce", OP.add, replica_groups=[list(range(NCORE))],
                ins=[arin[u].opt()], outs=[arout[u].opt()]),
                rs=(sl_arin[u],), ws=(sl_arout[u],))

        def pc_arr(blk, half):
            u = 2 * blk + half
            bld.dS(lambda e, u=u: e.dma_start(
                out=sb_ssmp.ap().rearrange("p (g e) -> p g e", g=4),
                in_=arout[u].rearrange("(g p) e -> p g e", p=128)),
                rs=(sl_arout[u],), ws=(sl_ssmp,))
            if taps and blk == 0:
                bld.dS(lambda e, half=half: e.dma_start(
                    out=tap_t["ssmp0"].rearrange("(g p) e -> p g e", p=128)[:, 4 * half:4 * half + 4, :],
                    in_=sb_ssmp.ap().rearrange("p (g e) -> p g e", g=4)),
                    rs=(sl_ssmp,))

        def rstd(src_col, dst_col, scale, eps_col):
            bld.s(lambda e: e.activation(out=sb_st[:, dst_col:dst_col + 1],
                                         in_=sb_st[:, src_col:src_col + 1],
                                         func=AF.Ln, scale=scale,
                                         bias=sb_eps[:, eps_col:eps_col + 1]),
                  rs=(sl_st, sl_const), ws=(sl_st,))
            bld.s(lambda e: e.activation(out=sb_st[:, dst_col:dst_col + 1],
                                         in_=sb_st[:, dst_col:dst_col + 1],
                                         func=AF.Exp, scale=-0.5),
                  rs=(sl_st,), ws=(sl_st,))

        def pc_norm(blk, half, gl):
            g = half * 4 + gl
            base = gl * 288
            bld.s(lambda e, base=base: e.activation(
                out=sb_sq[:, 0:256], in_=sb_ssmp[:, base:base + 256], func=AF.Square,
                accum_out=sb_st[:, 0:1]),
                rs=(sl_ssmp,), ws=(sl_sq, sl_st))
            rstd(0, 2, 1.0 / 256.0, 0)
            for o, c in ((256, 3), (272, 5)):
                bld.s(lambda e, base=base, o=o, c=c: e.activation(
                    out=sb_sq[:, 0:16], in_=sb_ssmp[:, base + o:base + o + 16],
                    func=AF.Square, accum_out=sb_st[:, c:c + 1]),
                    rs=(sl_ssmp,), ws=(sl_sq, sl_st))
            rstd(3, 7, 1.0 / 16.0, 0)
            rstd(5, 8, 1.0 / 16.0, 0)
            bld.v(lambda e, base=base: e.tensor_scalar(
                out=sb_dtr[:, :], in0=sb_ssmp[:, base:base + 256],
                scalar1=sb_st[:, 2:3], scalar2=None, op0=OP.mult),
                rs=(sl_ssmp, sl_st), ws=(sl_dtr,))
            for kt in range(2):
                bld.t(lambda e, kt=kt: e.transpose(
                    psum[:, 3072:3072 + 128], sb_dtr[:, kt * 128:(kt + 1) * 128],
                    sb_eye[:, :]),
                    rs=(sl_dtr, sl_const), ws=(sl_bank[6],))
                bld.s(lambda e, kt=kt, g=g: e.activation(
                    out=sb_dtrT[:, kt * TB + g * 128:kt * TB + (g + 1) * 128],
                    in_=psum[:, 3072:3072 + 128], func=AF.Copy),
                    rs=(sl_bank[6],), ws=(sl_dtrT[half],))
            bld.v(lambda e, base=base: e.tensor_scalar(
                out=sb_bc[:, 0:16], in0=sb_ssmp[:, base + 256:base + 272],
                scalar1=sb_st[:, 7:8], scalar2=None, op0=OP.mult),
                rs=(sl_ssmp, sl_st), ws=(sl_bc,))
            bld.v(lambda e, base=base: e.tensor_scalar(
                out=sb_bc[:, 16:32], in0=sb_ssmp[:, base + 272:base + 288],
                scalar1=sb_st[:, 8:9], scalar2=None, op0=OP.mult),
                rs=(sl_ssmp, sl_st), ws=(sl_bc,))
            bld.v(lambda e: e.tensor_mul(sb_bc[:, :], sb_bc[:, :], sb_bln[:, :]),
                  rs=(sl_const,), ws=(sl_bc,))
            for o, row in ((0, 0), (16, 32)):
                bld.t(lambda e, o=o: e.transpose(
                    psum[0:16, 3200:3200 + 128], sb_bc[:, o:o + 16], sb_eye[:, :]),
                    rs=(sl_bc, sl_const), ws=(sl_bank[6],))
                bld.s(lambda e, g=g, row=row: e.activation(
                    out=sb_bct[row:row + 16, g * 128:(g + 1) * 128],
                    in_=psum[0:16, 3200:3200 + 128], func=AF.Copy),
                    rs=(sl_bank[6],), ws=(sl_bct[half],))

        def pc_dtp_mm(blk, half, mt):
            pre = 2 + (mt % 2)
            fns = []
            for kt in range(2):
                fns.append(lambda e, kt=kt, mt=mt, half=half, pre=pre: e.matmul(
                    BK(pre),
                    sb_dtw[:, kt * DC + mt * 128:kt * DC + (mt + 1) * 128],
                    sb_dtrT[:, kt * TB + half * HB:kt * TB + (half + 1) * HB],
                    start=(kt == 0), stop=(kt == 1)))
            bld.tgroup(fns, rs=(sl_dtrT[half], sl_const), ws=(sl_bank[pre],))

        def pc_dtp_rest(blk, half, mt):
            pre, scr, scr2 = 2 + (mt % 2), 4, 5
            par = mt % 2
            sp1 = sb_sp1[:, par * HB:(par + 1) * HB]
            bld.s(lambda e, mt=mt, sp1=sp1, pre=pre: e.activation(
                out=sp1, in_=BK(pre), func=AF.Relu,
                bias=sb_dtbias[:, mt:mt + 1]),
                rs=(sl_bank[pre], sl_const), ws=(sl_sp1[par],))
            bld.s(lambda e, mt=mt, pre=pre, scr=scr: e.activation(
                out=BK(scr), in_=BK(pre), func=AF.Abs,
                bias=sb_dtbias[:, mt:mt + 1]),
                rs=(sl_bank[pre], sl_const), ws=(sl_bank[scr],))
            bld.s(lambda e, scr=scr, scr2=scr2: e.activation(out=BK(scr2), in_=BK(scr),
                                                             func=AF.Exp, scale=-1.0),
                  rs=(sl_bank[scr],), ws=(sl_bank[scr2],))
            bld.s(lambda e, scr=scr, scr2=scr2: e.activation(out=BK(scr), in_=BK(scr2),
                                                             func=AF.Ln,
                                                             bias=sb_eps[:, 2:3]),
                  rs=(sl_bank[scr2], sl_const), ws=(sl_bank[scr],))
            bld.v(lambda e, mt=mt, sp1=sp1, scr=scr: e.tensor_add(
                sb_dt[:, mt * HB:(mt + 1) * HB], sp1, BK(scr)),
                rs=(sl_sp1[par], sl_bank[scr]), ws=(sl_dt[mt],))
            bld.v(lambda e, mt=mt, half=half: e.tensor_mul(
                sb_dtx[:, mt * HB:(mt + 1) * HB],
                sb_dt[:, mt * HB:(mt + 1) * HB],
                sb_xc[:, mt * TB + half * HB:mt * TB + (half + 1) * HB]),
                rs=(sl_dt[mt], sl_xc[mt][half]), ws=(sl_dtx[mt],))

        def pc_dtp(blk, half, mt):
            pc_dtp_mm(blk, half, mt)
            pc_dtp_rest(blk, half, mt)

        def make_A(blk, half, with_inproj_blk=None, with_shift=False):
            early = []
            if with_inproj_blk is not None:
                early.append(lambda: prefetch_w(0))
            for d in range(DT):
                early.append(lambda d=d: pc_conv(blk, half, d))
                if with_shift:
                    early.append(lambda d=d: pc_shift(d))
                if with_inproj_blk is not None:
                    early.append(lambda d=d: emit_inproj_mgroup(with_inproj_blk, d))
            if with_inproj_blk is not None:
                for m in range(8, 16):
                    early.append(lambda m=m: emit_inproj_mgroup(with_inproj_blk, m))
            for tt in range(4):
                early.append(lambda tt=tt: pc_xproj(blk, half, tt))
            early.append(lambda: pc_arw(blk, half))
            late = []
            mid = [lambda: pc_ar(blk, half)]
            late.append(lambda: pc_arr(blk, half))
            for gl in range(4):
                late.append(lambda gl=gl: pc_norm(blk, half, gl))
            dtp = []
            for mt in range(DT + 2):
                if mt >= 2:
                    dtp.append(lambda mt=mt: pc_dtp_rest(blk, half, mt - 2))
                if mt < DT:
                    dtp.append(lambda mt=mt: pc_dtp_mm(blk, half, mt))
            late += dtp
            return early, mid, late

        def run_A(ql):
            early, mid, late = ql
            for f in early + mid + late:
                f()

        def emit_bcast(blk, half, n):
            par = n % 3
            for rows, sb_rep, sl_rep in ((0, sb_brep, sl_brep), (32, sb_crep, sl_crep)):
                bld.tgroup([lambda e, rows=rows, n=n, half=half: e.matmul(
                    psum[:, 3584:3584 + HB],
                    sb_bcsel[rows:rows + 16, n * 128:(n + 1) * 128],
                    sb_bct[rows:rows + 16, half * HB:(half + 1) * HB],
                    start=True, stop=True)],
                    rs=(sl_bct[half], sl_const), ws=(sl_bank[7],))
                bld.s(lambda e, par=par, sb_rep=sb_rep: e.activation(
                    out=sb_rep[:, par * HB:(par + 1) * HB], in_=psum[:, 3584:3584 + HB],
                    func=AF.Copy),
                    rs=(sl_bank[7],), ws=(sl_rep[par],))

        def emit_scan(blk, half, queue=None):
            early, mid, late = queue if queue is not None else ([], [], [])
            queue_rest = []
            bld.dA(lambda e, blk=blk, half=half: e.dma_start(
                out=sb_zrd.ap().rearrange("p (d t) -> p d t", d=DT),
                in_=zsil[blk, :, :, half * HB:(half + 1) * HB].rearrange("d p t -> p d t")),
                rs=tuple(sl_zsil[blk]), ws=tuple(sl_zrd))
            emit_bcast(blk, half, 0)
            emit_bcast(blk, half, 1)

            if not PIPE_INTERLEAVE:
                for f in early:
                    f()
                early = []
            elif len(early) > PIPE_K:
                late = early[PIPE_K:] + late
                early = early[:PIPE_K]
            it = 0
            DRAIN_N = 10
            for n in range(NST):
                if n == 14:
                    for f in mid:
                        f()
                    mid = []
                rem = max(1, DRAIN_N - n)
                take = (len(early) + rem - 1) // rem if early else 0
                for _ in range(take):
                    early.pop(0)()
                npar = n % 3
                for mt in range(DT):
                    col = mt * 16 + n
                    apar = it % 2
                    hpar = it % 3
                    a_ap = sb_a[:, apar * HB:(apar + 1) * HB]
                    h_ap = sb_h[:, hpar * HB:(hpar + 1) * HB]
                    g_ap = sb_g[:, hpar * HB:(hpar + 1) * HB]
                    bld.s(lambda e, mt=mt, col=col, a_ap=a_ap: e.activation(
                        out=a_ap, in_=sb_dt[:, mt * HB:(mt + 1) * HB], func=AF.Exp,
                        scale=sb_A[:, col:col + 1]),
                        rs=(sl_dt[mt], sl_const), ws=(sl_a[apar],))
                    bld.v(lambda e, mt=mt, npar=npar: e.tensor_mul(
                        sb_b[:, :], sb_dtx[:, mt * HB:(mt + 1) * HB],
                        sb_brep[:, npar * HB:(npar + 1) * HB]),
                        rs=(sl_dtx[mt], sl_brep[npar]), ws=(sl_b,))
                    bld.v(lambda e, col=col, a_ap=a_ap, h_ap=h_ap: e.tensor_tensor_scan(
                        out=h_ap, data0=a_ap, data1=sb_b[:, :],
                        initial=sb_hcar[:, col:col + 1], op0=OP.mult, op1=OP.add),
                        rs=(sl_a[apar], sl_b, sl_hcar), ws=(sl_h[hpar],))
                    bld.v(lambda e, col=col, h_ap=h_ap: e.tensor_copy(
                        sb_hcar[:, col:col + 1], h_ap[:, HB - 1:HB]),
                        rs=(sl_h[hpar],), ws=(sl_hcar,))
                    bld.g(lambda e, npar=npar, h_ap=h_ap, g_ap=g_ap: e.tensor_tensor(
                        g_ap, h_ap, sb_crep[:, npar * HB:(npar + 1) * HB], OP.mult),
                        rs=(sl_h[hpar], sl_crep[npar]), ws=(sl_g[hpar],))
                    acc = bld.g if YACC_G else bld.v
                    if n == 0:
                        acc(lambda e, mt=mt, g_ap=g_ap: e.tensor_copy(
                            sb_y[:, mt * HB:(mt + 1) * HB], g_ap),
                            rs=(sl_g[hpar],), ws=(sl_y[mt],))
                    else:
                        acc(lambda e, mt=mt, g_ap=g_ap: e.tensor_tensor(
                            sb_y[:, mt * HB:(mt + 1) * HB],
                            sb_y[:, mt * HB:(mt + 1) * HB], g_ap, OP.add),
                            rs=(sl_g[hpar],), ws=(sl_y[mt],))
                    it += 1
                if n + 2 < NST:
                    emit_bcast(blk, half, n + 2)
            for f in early:
                f()
            queue_rest = late
            if taps and blk == 0 and half == 0:
                bld.dS(lambda e: e.dma_start(
                    out=tap_t["y0"].rearrange("(d p) t -> p d t", p=128),
                    in_=sb_y.ap().rearrange("p (d t) -> p d t", d=DT)),
                    rs=tuple(sl_y))
            return queue_rest

        def emit_y2y3(blk, half):
            for mt in range(DT):
                bld.v(lambda e, mt=mt, half=half: e.scalar_tensor_tensor(
                    out=sb_y[:, mt * HB:(mt + 1) * HB],
                    in0=sb_xc[:, mt * TB + half * HB:mt * TB + (half + 1) * HB],
                    scalar=sb_Dp[:, mt:mt + 1],
                    in1=sb_y[:, mt * HB:(mt + 1) * HB],
                    op0=OP.mult, op1=OP.add),
                    rs=(sl_xc[mt][half], sl_const), ws=(sl_y[mt],))
                bld.v(lambda e, mt=mt: e.tensor_mul(
                    sb_zrd[:, mt * HB:(mt + 1) * HB],
                    sb_y[:, mt * HB:(mt + 1) * HB],
                    sb_zrd[:, mt * HB:(mt + 1) * HB]),
                    rs=(sl_y[mt],), ws=(sl_zrd[mt],))
            for tl in range(4):
                bld.dS(lambda e, blk=blk, half=half, tl=tl: e.dma_start(
                    out=y3a_in[blk, 4 * half + tl].rearrange("(d p) c -> p d c", p=128),
                    in_=sb_zrd.ap().rearrange("p (d t) -> p d t", d=DT)[:, :, tl * 128:(tl + 1) * 128]),
                    rs=tuple(sl_zrd), ws=(sl_y3in[blk],))
            if taps and blk == 0:
                bld.dS(lambda e, half=half: e.dma_start(
                    out=tap_t["dt0"].rearrange("(d p) t -> p d t", p=128)[:, :, half * HB:(half + 1) * HB],
                    in_=sb_dt.ap().rearrange("p (d t) -> p d t", d=DT)),
                    rs=tuple(sl_dt))
                bld.dS(lambda e, half=half: e.dma_start(
                    out=tap_t["xc0"].rearrange("(d p) t -> p d t", p=128)[:, :, half * HB:(half + 1) * HB],
                    in_=sb_xc.ap().rearrange("p (d t) -> p d t", d=DT)[:, :, half * HB:(half + 1) * HB]),
                    rs=tuple(sl_xc[d][half] for d in range(DT)))

        def emit_a2a(blk):
            with allow_ici_all_to_all():
                bld.g(lambda e, blk=blk: e.collective_compute(
                    "AllToAll", OP.bypass, replica_groups=[list(range(NCORE))],
                    ins=[y3a_in[blk].rearrange("r d c -> (r d) c").opt()],
                    outs=[y3a_out[blk].rearrange("r d c -> (r d) c").opt()]),
                    rs=(sl_y3in[blk],), ws=(sl_y3out[blk],))
            p = blk // 4
            mloc = blk % 4
            bld.dS(lambda e, blk=blk, p=p, mloc=mloc: e.dma_start(
                out=y3p[p][:, mloc * 128:(mloc + 1) * 128].rearrange("(k q) c -> q k c", q=128),
                in_=y3a_out[blk].rearrange("r (k q) c -> q (r k) c", q=128)),
                rs=(sl_y3out[blk],), ws=(sl_y3p[p],))
            if taps and blk == 0:
                bld.dS(lambda e: e.dma_start(
                    out=tap_t["y30"].rearrange("d (tl c) -> d tl c", tl=8),
                    in_=y3a_in[0].rearrange("tl d c -> d tl c")),
                    rs=(sl_y3in[0],))
                bld.dS(lambda e: e.dma_start(
                    out=tap_t["y3o"][:, :],
                    in_=y3a_out[0].rearrange("r d c -> (r d) c")),
                    rs=(sl_y3out[0],))

        def emit_ln(xbuf, s_x, st2, s_st2, trash, s_trash, eps):
            bld.v(lambda e: e.tensor_reduce(st2[:, 0:1], xbuf[:, :], AX, OP.add),
                  rs=(s_x,), ws=(s_st2,))
            bld.s(lambda e: e.activation(out=trash[:, :], in_=xbuf[:, :], func=AF.Square,
                                         accum_out=st2[:, 2:3]),
                  rs=(s_x,), ws=(s_trash, s_st2))
            bld.v(lambda e: e.tensor_scalar(out=st2[:, 4:5], in0=st2[:, 0:1],
                                            scalar1=1.0 / H, scalar2=None, op0=OP.mult),
                  rs=(s_st2,), ws=(s_st2,))
            bld.v(lambda e: e.tensor_scalar(out=st2[:, 1:2], in0=st2[:, 4:5],
                                            scalar1=-1.0, scalar2=None, op0=OP.mult),
                  rs=(s_st2,), ws=(s_st2,))
            bld.v(lambda e: e.tensor_mul(st2[:, 5:6], st2[:, 4:5], st2[:, 4:5]),
                  rs=(s_st2,), ws=(s_st2,))
            bld.v(lambda e: e.tensor_scalar(out=st2[:, 6:7], in0=st2[:, 2:3],
                                            scalar1=1.0 / H, scalar2=None, op0=OP.mult),
                  rs=(s_st2,), ws=(s_st2,))
            bld.v(lambda e: e.tensor_tensor(st2[:, 7:8], st2[:, 6:7], st2[:, 5:6],
                                            OP.subtract),
                  rs=(s_st2,), ws=(s_st2,))
            bld.s(lambda e: e.activation(out=st2[:, 8:9], in_=st2[:, 7:8],
                                         func=AF.Ln, bias=sb_eps[:, 1:2]),
                  rs=(s_st2, sl_const), ws=(s_st2,))
            bld.s(lambda e: e.activation(out=st2[:, 9:10], in_=st2[:, 8:9],
                                         func=AF.Exp, scale=-0.5),
                  rs=(s_st2,), ws=(s_st2,))
            bld.v(lambda e: e.tensor_scalar(out=xbuf[:, :], in0=xbuf[:, :],
                                            scalar1=st2[:, 1:2], scalar2=st2[:, 9:10],
                                            op0=OP.add, op1=OP.mult),
                  rs=(s_st2,), ws=(s_x,))

        def bank_deps_all():
            d = []
            for s in sl_bank:
                d.extend(s.w)
                d.extend(s.r)
            return d

        def bank_claim(tok):
            for s in sl_bank:
                s.w = [tok]
                s.r = []

        def emit_pass(p):
            bld.fence()
            free_inproj()
            pst = contextlib.ExitStack()
            y3t = [pst.enter_context(nc.sbuf_tensor(f"pa_y3t{i}_{p}", [128, 512], BF16)) for i in range(3)]
            wt = [pst.enter_context(nc.sbuf_tensor(f"pa_wt{i}_{p}", [128, 1024], BF16)) for i in range(3)]
            ascr = [pst.enter_context(nc.sbuf_tensor(f"pa_as{i}_{p}", [128, 1024], BF16)) for i in range(2)]
            s_y3t = [S() for _ in range(3)]
            s_wt = [S() for _ in range(3)]
            s_ascr = [S(), S()]


            def matpass(nkt, lhs_src_dma, rhs_src_dma):
                for ng in range(4):
                    for kt in range(nkt):
                        sl2 = kt % 3
                        lhs_src_dma(kt, sl2)
                        rhs_src_dma(kt, ng, sl2)
                        fns = []
                        for mi in range(4):
                            for nc2 in range(2):
                                fns.append(lambda e, mi=mi, nc2=nc2, kt=kt, sl2=sl2: e.matmul(
                                    BK(mi * 2 + nc2),
                                    y3t[sl2][:, mi * 128:(mi + 1) * 128],
                                    wt[sl2][:, nc2 * 512:(nc2 + 1) * 512],
                                    start=(kt == 0), stop=(kt == nkt - 1)))
                        extra = bank_deps_all() if kt == 0 else ()
                        tok = bld.tgroup(fns, rs=(s_y3t[sl2], s_wt[sl2]), deps=extra)
                        if kt == nkt - 1:
                            bank_claim(tok)
                    yield ng

            def op_lhs(kt, sl2):
                bld.dS(lambda e, kt=kt, sl2=sl2: e.dma_start(
                    out=y3t[sl2][:, :], in_=y3p[p][kt * 128:(kt + 1) * 128, :]),
                    rs=(sl_y3p[p],), ws=(s_y3t[sl2],))

            def op_rhs(kt, ng, sl2):
                bld.dG(lambda e, kt=kt, ng=ng, sl2=sl2: e.dma_start(
                    out=wt[sl2][:, :],
                    in_=wout[kt * 128:(kt + 1) * 128, ng * 1024:(ng + 1) * 1024]),
                    ws=(s_wt[sl2],))

            for ng in matpass(64, op_lhs, op_rhs):
                for mi in range(4):
                    par = mi % 2
                    for nc2 in range(2):
                        bld.s(lambda e, mi=mi, nc2=nc2, par=par: e.activation(
                            out=ascr[par][:, nc2 * 512:(nc2 + 1) * 512],
                            in_=BK(mi * 2 + nc2), func=AF.Copy),
                            rs=(sl_bank[mi * 2 + nc2],), ws=(s_ascr[par],))
                    bld.dA(lambda e, mi=mi, ng=ng, par=par: e.dma_start(
                        out=attnp[mi][:, ng * 1024:(ng + 1) * 1024], in_=ascr[par][:, :]),
                        rs=(s_ascr[par],), ws=(sl_attnp[mi],))

            prep_st.close()

            if taps and p == 0:
                bld.dS(lambda e: e.dma_start(out=tap_t["at0"][:, :], in_=attnp[0][:, :]),
                       rs=(sl_attnp[0],))

            resb2 = [pst.enter_context(nc.sbuf_tensor(f"pa_res{i}_{p}", [128, H], F32)) for i in range(2)]
            atb2 = [pst.enter_context(nc.sbuf_tensor(f"pa_at{i}_{p}", [128, H], BF16)) for i in range(2)]
            s_res2 = [S(), S()]
            s_at2 = [S(), S()]
            xpre = pst.enter_context(nc.sbuf_tensor(f"pa_xp_{p}", [128, H], F32))
            st2 = pst.enter_context(nc.sbuf_tensor(f"pa_st_{p}", [128, 16], F32))
            x1t = pst.enter_context(nc.sbuf_tensor(f"pa_x1t_{p}", [128, KTH * 128], BF16))
            s_xp, s_st2, s_x1t = S(), S(), S()
            for mi in range(4):
                blk = p * 4 + mi
                par = mi % 2
                resb, atb = resb2[par], atb2[par]
                s_res, s_at = s_res2[par], s_at2[par]
                bld.dA(lambda e, blk=blk, resb=resb: e.dma_start(
                    out=resb[:, :], in_=resid[blk * 128:(blk + 1) * 128, :]),
                    ws=(s_res,))
                bld.dA(lambda e, mi=mi, atb=atb: e.dma_start(out=atb[:, :], in_=attnp[mi][:, :]),
                       rs=(sl_attnp[mi],), ws=(s_at,))
                bld.v(lambda e, resb=resb, atb=atb: e.tensor_add(xpre[:, :], resb[:, :], atb[:, :]),
                      rs=(s_res, s_at), ws=(s_xp,))
                emit_ln(xpre, s_xp, st2, s_st2, atb, s_at, EPS_LN)
                if taps and p == 0 and mi == 0:
                    bld.dS(lambda e: e.dma_start(out=tap_t["x10"][:, :], in_=xpre[:, :]),
                           rs=(s_xp,))
                bld.dA(lambda e, mi=mi: e.dma_start(out=x1sp[mi][:, :], in_=xpre[:, :]),
                       rs=(s_xp,), ws=(sl_x1sp[mi],))
                for kt in range(KTH):
                    bank6_off = 3072 + (kt % 2) * 128
                    bld.t(lambda e, kt=kt, o=bank6_off: e.transpose(
                        psum[:, o:o + 128], xpre[:, kt * 128:(kt + 1) * 128],
                        sb_eye[:, :]),
                        rs=(s_xp, sl_const), ws=(sl_bank[6],))
                    bld.s(lambda e, kt=kt, o=bank6_off: e.activation(
                        out=x1t[:, kt * 128:(kt + 1) * 128], in_=psum[:, o:o + 128],
                        func=AF.Copy),
                        rs=(sl_bank[6],), ws=(s_x1t,))
                bld.dA(lambda e, mi=mi: e.dma_start(
                    out=x1p[:, mi * 128:(mi + 1) * 128].rearrange("(k q) c -> q k c", q=128),
                    in_=x1t.ap().rearrange("q (k c) -> q k c", k=KTH)),
                    rs=(s_x1t,), ws=tuple(sl_x1p))

            x2scr = pst.enter_context(nc.sbuf_tensor(f"pa_x2_{p}", [128, 1024], F32))
            ffnb_sb = pst.enter_context(nc.sbuf_tensor(f"pa_fb_{p}", [128, H], BF16))
            s_x2, s_fb = S(), S()
            bld.dG(lambda e: e.dma_start(out=ffnb_sb[:, :], in_=ffnb_in[:, :]), ws=(s_fb,))

            def ff_lhs(kt, sl2):
                bld.dS(lambda e, kt=kt, sl2=sl2: e.dma_start(
                    out=y3t[sl2][:, :], in_=x1p[kt * 128:(kt + 1) * 128, :]),
                    rs=(sl_x1p[kt],), ws=(s_y3t[sl2],))

            def ff_rhs(kt, ng, sl2):
                bld.dS(lambda e, kt=kt, ng=ng, sl2=sl2: e.dma_start(
                    out=wt[sl2][:, :],
                    in_=wffn_bf[kt * 128:(kt + 1) * 128, ng * 1024:(ng + 1) * 1024]),
                    rs=(sl_wffn[kt],), ws=(s_wt[sl2],))

            for ng in matpass(KTH, ff_lhs, ff_rhs):
                if taps and p == 0:
                    bld.s(lambda e: e.activation(out=x2scr[:, 0:512], in_=BK(0), func=AF.Copy),
                          rs=(sl_bank[0],), ws=(s_x2,))
                    bld.dA(lambda e, ng=ng: e.dma_start(out=tap_t["ps0"][ng], in_=x2scr[:, 0:512]),
                           rs=(s_x2,))
                for mi in range(4):
                    for nc2 in range(2):
                        bld.v(lambda e, mi=mi, nc2=nc2, ng=ng: e.tensor_add(
                            x2scr[:, nc2 * 512:(nc2 + 1) * 512], BK(mi * 2 + nc2),
                            ffnb_sb[:, ng * 1024 + nc2 * 512:ng * 1024 + (nc2 + 1) * 512]),
                            rs=(sl_bank[mi * 2 + nc2], s_fb), ws=(s_x2,))
                    bld.dA(lambda e, mi=mi, ng=ng: e.dma_start(
                        out=x2p[mi][:, ng * 1024:(ng + 1) * 1024], in_=x2scr[:, :]),
                        rs=(s_x2,), ws=(sl_x2p[mi],))

            if taps and p == 0:
                bld.dS(lambda e: e.dma_start(out=tap_t["wb0"][:, :], in_=wffn_bf[0:256, :]),
                       deps=[t for k in range(2) for t in sl_wffn[k].w])
                bld.dS(lambda e: e.dma_start(out=tap_t["wbl"][:, :], in_=wffn_bf[3840:4096, :]),
                       deps=[t for k in (30, 31) for t in sl_wffn[k].w])
                bld.dS(lambda e: e.dma_start(out=tap_t["x1pl"][:, :], in_=x1p[3840:4096, :]),
                       deps=[t for k in (30, 31) for t in sl_x1p[k].w])
                bld.dS(lambda e: e.dma_start(out=tap_t["x2p0"][:, :], in_=x2p[0][:, :]),
                       rs=(sl_x2p[0],))

            for mi in range(4):
                blk = p * 4 + mi
                par = mi % 2
                resb, atb = resb2[par], atb2[par]
                s_res, s_at = s_res2[par], s_at2[par]
                bld.dA(lambda e, mi=mi: e.dma_start(out=xpre[:, :], in_=x2p[mi][:, :]),
                       rs=(sl_x2p[mi],), ws=(s_xp,))
                bld.dA(lambda e, mi=mi, resb=resb: e.dma_start(out=resb[:, :], in_=x1sp[mi][:, :]),
                       rs=(sl_x1sp[mi],), ws=(s_res,))
                bld.v(lambda e, resb=resb: e.tensor_add(xpre[:, :], xpre[:, :], resb[:, :]),
                      rs=(s_res,), ws=(s_xp,))
                emit_ln(xpre, s_xp, st2, s_st2, atb, s_at, EPS_LN)
                bld.dA(lambda e, blk=blk: e.dma_start(
                    out=out[blk * 128:(blk + 1) * 128, :], in_=xpre[:, :]),
                    rs=(s_xp,))
            pst.close()
            bld.fence()

        with contextlib.ExitStack() as prep_st:
            wfa = prep_st.enter_context(nc.sbuf_tensor("pr_wfa", [128, H], F32))
            wfb = prep_st.enter_context(nc.sbuf_tensor("pr_wfb", [128, H], F32))
            wfc = prep_st.enter_context(nc.sbuf_tensor("pr_wfc", [128, H], BF16))
            s_wfa, s_wfb, s_wfc = S(), S(), S()
            for kt in range(KTH):
                bld.dS(lambda e, kt=kt: e.dma_start(
                    out=wfa[:, :], in_=wffnT[kt * 128:(kt + 1) * 128, :]),
                    ws=(s_wfa,))
                bld.s(lambda e: e.activation(out=wfb[:, :], in_=wfa[:, :],
                                             func=AF.Tanh, scale=sb_kk[:, 0:1]),
                      rs=(s_wfa, sl_const), ws=(s_wfb,))
                bld.v(lambda e: e.tensor_scalar(out=wfc[:, :], in0=wfb[:, :],
                                                scalar1=sb_aa[:, 0:1], scalar2=None,
                                                op0=OP.mult),
                      rs=(s_wfb, sl_const), ws=(s_wfc,))
                bld.dS(lambda e, kt=kt: e.dma_start(
                    out=wffn_bf[kt * 128:(kt + 1) * 128, :], in_=wfc[:, :]),
                    rs=(s_wfc,), ws=(sl_wffn[kt],))
            bld.fence()

        alloc_inproj()
        load_hT(0)
        bld.v(lambda e: e.memset(sb_hcar[:, :], 0.0), ws=(sl_hcar,))
        emit_inproj(0)
        run_A(make_A(0, 0))
        for blk in range(NBLK):
            if blk == 4:
                bld.v(lambda e: e.memset(sb_hcar[:, :], 0.0), ws=(sl_hcar,))
            boundary = blk in (3, 7)
            if not boundary:
                load_hT(blk + 1)
            q0 = make_A(blk, 1,
                        with_inproj_blk=(blk + 1) if not boundary else None,
                        with_shift=(not boundary and (blk + 1) % 4 != 0))
            rest0 = emit_scan(blk, 0, q0)
            emit_y2y3(blk, 0)
            for f in rest0:
                f()
            q1 = make_A(blk + 1, 0) if not boundary else None
            rest1 = emit_scan(blk, 1, q1)
            emit_y2y3(blk, 1)
            emit_a2a(blk)
            for f in rest1:
                f()
            if blk == 3:
                emit_pass(0)
                alloc_inproj()
                load_hT(4)
                emit_inproj(4)
                run_A(make_A(4, 0))
            elif blk == 7:
                emit_pass(1)

        bld.fence()

        _last_bld[0] = {e: list(bld.meta[e]) for e in ENGINES}
        with nc.Block() as block:
            @block.sync
            def _(e):
                for f in bld.ops["sync"]:
                    f(e)

            @block.gpsimd
            def _(e):
                for f in bld.ops["gpsimd"]:
                    f(e)

            @block.scalar
            def _(e):
                for f in bld.ops["scalar"]:
                    f(e)

            @block.vector
            def _(e):
                for f in bld.ops["vector"]:
                    f(e)

            @block.tensor
            def _(e):
                for f in bld.ops["tensor"]:
                    f(e)

    return nc


_last_bld = [None]


def host_inputs(core, inputs):
    hid = np.ascontiguousarray(inputs["L0_attn_hidden_states"].reshape(NBLK * TB, H))
    sl = slice(core * DC, (core + 1) * DC)
    W = inputs["L0_attn_in_proj_w"]
    win = np.ascontiguousarray(np.concatenate(
        [W[:, core * DC:(core + 1) * DC],
         W[:, 2 * H + core * DC:2 * H + (core + 1) * DC]], axis=1))
    rows = np.concatenate([np.arange(blk * TB + core * 128, blk * TB + (core + 1) * 128)
                           for blk in range(NBLK)])
    resid = np.ascontiguousarray(hid[rows])
    convw = np.ascontiguousarray(
        inputs["L0_attn_conv_w"][sl].reshape(DT, 128, 4).transpose(1, 0, 2).reshape(128, DT * 4))
    convb = np.ascontiguousarray(inputs["L0_attn_conv_b"][sl].reshape(DT, 128).T)
    dtbias = np.ascontiguousarray(inputs["L0_attn_dt_bias"][sl].reshape(DT, 128).T)
    Dp = np.ascontiguousarray(inputs["L0_attn_D_param"][sl].reshape(DT, 128).T)
    Alog = np.ascontiguousarray(
        inputs["L0_attn_A_log"][sl].reshape(DT, 128, NST).transpose(1, 0, 2).reshape(128, 128))
    xpw = np.ascontiguousarray(inputs["L0_attn_x_proj_w"][:, sl].T)
    dtw = np.ascontiguousarray(inputs["L0_attn_dt_proj_w"][sl].T)
    dtlnw = np.ascontiguousarray(inputs["L0_attn_dt_ln_w"].reshape(2, 128).T)
    bcsel = np.zeros((48, 2048), np.float32)
    for n in range(NST):
        bcsel[n, n * 128:(n + 1) * 128] = 1.0
        bcsel[32 + n, n * 128:(n + 1) * 128] = 1.0
    bln = np.ascontiguousarray(np.broadcast_to(
        np.concatenate([inputs["L0_attn_B_ln_w"], inputs["L0_attn_C_ln_w"]])[None, :],
        (128, 32)))
    ffnb = np.ascontiguousarray(np.broadcast_to(inputs["L0_ffn_bias"][None, :], (128, H)))
    kk = np.ascontiguousarray(np.broadcast_to(
        np.asarray(inputs["L0_ffn_kk"], np.float32).reshape(1, 1), (128, 1)))
    aa = np.ascontiguousarray(np.broadcast_to(
        np.asarray(inputs["L0_ffn_aa"], np.float32).reshape(1, 1), (128, 1)))
    return {
        "resid": resid, "win": win,
        "convw": convw, "convb": convb, "dtbias": dtbias, "Dp": Dp, "Alog": Alog,
        "xpw": xpw, "dtw": dtw, "dtlnw": dtlnw, "bcsel": bcsel, "blnrep": bln,
        "ffnb": ffnb, "kk": kk, "aa": aa,
    }


def kernel(**inputs):
    hid = np.ascontiguousarray(inputs["L0_attn_hidden_states"].reshape(NBLK * TB, H))
    hT = np.ascontiguousarray(hid.T)
    wout = np.ascontiguousarray(inputs["L0_attn_out_proj_w"].T)
    wffnT = np.ascontiguousarray(inputs["L0_ffn_weight"].T)

    nc = build_program(taps=DEBUG_TAPS)
    in_maps = []
    for core in range(NCORE):
        m = host_inputs(core, inputs)
        m["hT"] = hT
        m["wout"] = wout
        m["wffnT"] = wffnT
        in_maps.append(m)
    res = run_bass_kernel_spmd(nc, in_maps, core_ids=list(range(NCORE)))

    out_full = np.empty((NBLK * TB, H), np.float32)
    for core in range(NCORE):
        o = res.results[core]["out"]
        for blk in range(NBLK):
            out_full[blk * TB + core * 128: blk * TB + (core + 1) * 128] = \
                o[blk * 128:(blk + 1) * 128]
    if DEBUG_TAPS:
        kernel.last_results = res
    return out_full.reshape(2, NBLK * TB // 2, H)
